# version 15 (final)
# speedup vs baseline: 2.4154x; 1.0190x over previous
from contextlib import ExitStack

import numpy as np
import ml_dtypes

import concourse.bass as bass
import concourse.mybir as mybir
from concourse.bass_utils import run_bass_kernel_spmd

F32 = mybir.dt.float32
BF16 = mybir.dt.bfloat16
AF = mybir.ActivationFunctionType
ALU = mybir.AluOpType
AX = mybir.AxisListType

D = 1024
T = 2048
NTS = T // 128
SEQ = 8192
NKB = SEQ // 128
EPS = 1e-5
N_EXP = 16
INTER = 512
L1H = 2048
RG = [[0, 1, 2, 3], [4, 5, 6, 7]]

_PAIRS = [(0, 1), (0, 2), (0, 3), (1, 2), (1, 3), (2, 3)]

ENGS = ("sync", "tensor", "scalar", "vector", "gpsimd")


class StepH(tuple):
    pass


DMA_R = 3


class Sched:

    def __init__(self, nc, sems, semval):
        self.nc = nc
        self.sems = sems
        self.semval = semval
        self.steps = {e: [] for e in ENGS}
        self.dmacnt = {e: 0 for e in ENGS}
        self.last = None
        self.entry = [(n, v) for n, v in semval.items() if v > 0]

    def op(self, eng, fn, deps=None):
        return self._add(eng, fn, deps, False)

    def dma(self, eng, fn, deps=None):
        return self._add(eng, fn, deps, True)

    def _add(self, eng, fn, deps, is_dma):
        if deps is None:
            deps = [self.last] if self.last is not None else []
        h = StepH((eng, len(self.steps[eng])))
        self.steps[eng].append({
            "fn": fn, "deps": [dd for dd in deps if dd is not None],
            "dma": is_dma, "dmaidx": self.dmacnt[eng] if is_dma else None})
        if is_dma:
            self.dmacnt[eng] += 1
        self.last = h
        return h

    def run(self):
        semof = {}
        counts = {}
        for e in ENGS:
            for i, st in enumerate(self.steps[e]):
                if st["dma"]:
                    sname = "d_%s%d" % (e, st["dmaidx"] % DMA_R)
                    counts[sname] = counts.get(sname, self.semval[sname]) + 16
                else:
                    sname = "z_" + e
                    counts[sname] = counts.get(sname, self.semval[sname]) + 1
                semof[(e, i)] = (sname, counts[sname])
        for e in ENGS:
            byidx = {st["dmaidx"]: i for i, st in enumerate(self.steps[e]) if st["dma"]}
            for i, st in enumerate(self.steps[e]):
                if st["dma"] and st["dmaidx"] - DMA_R >= 0:
                    st["deps"].append(semof[(e, byidx[st["dmaidx"] - DMA_R])])
        for e in ENGS:
            if self.steps[e]:
                self.steps[e][0]["deps"] = list(self.steps[e][0]["deps"]) + self.entry
        with self.nc.Block() as blk:
            for e in ENGS:
                if not self.steps[e]:
                    continue

                def make(e):
                    def emit(eng_obj):
                        hiw = {}
                        for i, st in enumerate(self.steps[e]):
                            for dd in st["deps"]:
                                sname, val = (semof[dd] if isinstance(dd, StepH) else dd)
                                if hiw.get(sname, -1) >= val:
                                    continue
                                eng_obj.wait_ge(self.sems[sname], val)
                                hiw[sname] = val
                            instr = st["fn"](eng_obj)
                            sname, val = semof[(e, i)]
                            instr.then_inc(self.sems[sname], 16 if st["dma"] else 1)
                    return emit
                getattr(blk, e)(make(e))
        for sname, v in counts.items():
            self.semval[sname] = v


def make_sems(nc, octx):
    sems, semval = {}, {}
    for e in ENGS:
        for n in ["z_" + e] + ["d_%s%d" % (e, i) for i in range(DMA_R)]:
            sems[n] = octx.enter_context(nc.semaphore(n))
            semval[n] = 0
    return sems, semval


def final_barrier(nc, sems, semval):
    with nc.Block() as blk:
        @blk.sync
        def _(s):
            for n, v in semval.items():
                if v > 0:
                    s.wait_ge(sems[n], v)


def _sb(nc, ctx, name, shape, dtype):
    return ctx.enter_context(nc.sbuf_tensor(name, list(shape), dtype))


def _ps(nc, ctx, name, shape, dtype=F32):
    return ctx.enter_context(nc.psum_tensor(name, list(shape), dtype))


def _ln(S, IN, TMP, OUT, SQBUF, COLS, c0, epscol):
    S.op("vector", lambda v: v.reduce_sum(COLS[:, c0:c0 + 1], IN[:, :], AX.X))
    S.op("scalar", lambda a: a.activation(COLS[:, c0 + 1:c0 + 2], COLS[:, c0:c0 + 1], AF.Copy, scale=1.0 / D))
    S.op("vector", lambda v: v.tensor_single_scalar(TMP[:, :], IN[:, :], COLS[:, c0 + 1:c0 + 2], ALU.subtract))
    S.op("scalar", lambda a: a.activation(SQBUF[:, 0:D], TMP[:, :], AF.Square))
    S.op("vector", lambda v: v.reduce_sum(COLS[:, c0 + 2:c0 + 3], SQBUF[:, 0:D], AX.X))
    S.op("scalar", lambda a: a.activation(COLS[:, c0 + 3:c0 + 4], COLS[:, c0 + 2:c0 + 3], AF.Copy, scale=1.0 / D))
    S.op("scalar", lambda a: a.activation(COLS[:, c0 + 4:c0 + 5], COLS[:, c0 + 3:c0 + 4], AF.Sqrt, bias=epscol))
    S.op("vector", lambda v: v.reciprocal(COLS[:, c0 + 5:c0 + 6], COLS[:, c0 + 4:c0 + 5]))
    S.op("vector", lambda v: v.tensor_single_scalar(OUT[:, :], TMP[:, :], COLS[:, c0 + 5:c0 + 6], ALU.mult))



def emit_p1(nc, d, sems, semval):
    with ExitStack() as ctx:
        CONSTS = _sb(nc, ctx, "p1_consts", [128, 2], F32)
        QWS = _sb(nc, ctx, "p1_qws", [128, 3 * D], F32)
        QW = _sb(nc, ctx, "p1_qw", [128, 8 * 3 * D], BF16)
        LNW = _sb(nc, ctx, "p1_lnw", [128, D], F32)
        LNB = _sb(nc, ctx, "p1_lnb", [128, D], F32)
        QB = _sb(nc, ctx, "p1_qb", [128, 3 * D], F32)
        X = _sb(nc, ctx, "p1_x", [128, D], F32)
        TMP = _sb(nc, ctx, "p1_tmp", [128, D], F32)
        XN = _sb(nc, ctx, "p1_xn", [128, D], F32)
        H = _sb(nc, ctx, "p1_h", [128, D], F32)
        HT = _sb(nc, ctx, "p1_ht", [128, D], BF16)
        QKV = _sb(nc, ctx, "p1_qkv", [128, 3 * D], F32)
        SQ = _sb(nc, ctx, "p1_sq", [128, 2 * D], F32)
        NRM = _sb(nc, ctx, "p1_nrm", [128, 96], F32)
        COLS = _sb(nc, ctx, "p1_cols", [128, 16], F32)
        IDENT = _sb(nc, ctx, "p1_ident", [128, 128], F32)
        TBF = _sb(nc, ctx, "p1_tbf", [128, D], BF16)
        PQ = _ps(nc, ctx, "p1_pq", [128, 3 * D])
        PT = _ps(nc, ctx, "p1_pt", [128, 512])

        S = Sched(nc, sems, semval)
        S.dma("sync", lambda s: s.dma_start(out=IDENT[:, :], in_=d["c_ident"].ap()))
        S.dma("sync", lambda s: s.dma_start(out=CONSTS[:, :], in_=d["c_consts"].ap()))
        S.dma("sync", lambda s: s.dma_start(out=LNW[:, :], in_=d["lnw_r"].ap()))
        S.dma("sync", lambda s: s.dma_start(out=LNB[:, :], in_=d["lnb_r"].ap()))
        S.dma("sync", lambda s: s.dma_start(out=QB[:, :], in_=d["qkvb_r"].ap()))
        for c in range(8):
            S.dma("sync", lambda s, c=c: s.dma_start(
                out=QWS[:, :], in_=d["qkvwT"].ap()[c * 128:(c + 1) * 128, :]))
            S.op("vector", lambda v, c=c: v.tensor_copy(
                QW[:, c * 3 * D:(c + 1) * 3 * D], QWS[:, :]))

        qT_r = d["qT0"].ap().rearrange("(c p) t -> p c t", p=128)
        kT_r = d["kT0"].ap().rearrange("(c p) t -> p c t", p=128)

        for ts in range(NTS):
            S.dma("sync", lambda s, ts=ts: s.dma_start(
                out=X[:, :], in_=d["x"].ap()[ts * 128:(ts + 1) * 128, :]))
            _ln(S, X, TMP, XN, SQ, COLS, 0, CONSTS[:, 0:1])
            S.dma("gpsimd", lambda s, ts=ts: s.dma_start(
                out=d["xn0"].ap()[ts * 128:(ts + 1) * 128, :], in_=XN[:, :]))
            _ln(S, XN, TMP, H, SQ, COLS, 6, CONSTS[:, 0:1])
            S.op("vector", lambda v: v.tensor_mul(H[:, :], H[:, :], LNW[:, :]))
            S.op("vector", lambda v: v.tensor_add(H[:, :], H[:, :], LNB[:, :]))

            for grp in range(2):
                for c in range(4):
                    S.op("tensor", lambda t, grp=grp, c=c: t.transpose(
                        PT[:, c * 128:(c + 1) * 128],
                        H[:, (grp * 4 + c) * 128:(grp * 4 + c + 1) * 128],
                        IDENT[:, :]))
                S.op("vector", lambda v, grp=grp: v.tensor_copy(
                    HT[:, grp * 512:(grp + 1) * 512], PT[:, :]))

            for nb in range(6):
                for c in range(8):
                    S.op("tensor", lambda t, nb=nb, c=c: t.matmul(
                        PQ[:, nb * 512:(nb + 1) * 512],
                        HT[:, c * 128:(c + 1) * 128],
                        QW[:, c * 3 * D + nb * 512: c * 3 * D + (nb + 1) * 512],
                        start=(c == 0), stop=(c == 7)))
            for nb in range(3):
                S.op("vector", lambda v, nb=nb: v.tensor_add(
                    QKV[:, nb * 1024:(nb + 1) * 1024],
                    PQ[:, nb * 1024:(nb + 1) * 1024],
                    QB[:, nb * 1024:(nb + 1) * 1024]))

            S.op("scalar", lambda a: a.activation(SQ[:, :], QKV[:, 0:2 * D], AF.Square))
            for hh in range(32):
                S.op("vector", lambda v, hh=hh: v.reduce_sum(
                    NRM[:, hh:hh + 1], SQ[:, hh * 64:(hh + 1) * 64], AX.X))
            S.op("scalar", lambda a: a.activation(NRM[:, 32:64], NRM[:, 0:32], AF.Sqrt))
            S.op("vector", lambda v: v.tensor_single_scalar(NRM[:, 32:64], NRM[:, 32:64], CONSTS[:, 1:2], ALU.max))
            S.op("vector", lambda v: v.reciprocal(NRM[:, 64:96], NRM[:, 32:64]))
            for hh in range(32):
                S.op("vector", lambda v, hh=hh: v.tensor_single_scalar(
                    QKV[:, hh * 64:(hh + 1) * 64], QKV[:, hh * 64:(hh + 1) * 64],
                    NRM[:, 64 + hh:65 + hh], ALU.mult))

            for name_r, off in ((qT_r, 0), (kT_r, D)):
                for grp in range(2):
                    for c in range(4):
                        S.op("tensor", lambda t, off=off, grp=grp, c=c: t.transpose(
                            PT[:, c * 128:(c + 1) * 128],
                            QKV[:, off + (grp * 4 + c) * 128: off + (grp * 4 + c + 1) * 128],
                            IDENT[:, :]))
                    S.op("vector", lambda v, grp=grp: v.tensor_copy(
                        TBF[:, grp * 512:(grp + 1) * 512], PT[:, :]))
                S.dma("gpsimd", lambda s, name_r=name_r, ts=ts: s.dma_start(
                    out=name_r[:, :, ts * 128:(ts + 1) * 128],
                    in_=TBF.ap().rearrange("p (c t) -> p c t", c=8)))

            S.op("vector", lambda v: v.tensor_copy(TBF[:, 0:D], QKV[:, 2 * D:3 * D]))
            S.dma("gpsimd", lambda s, ts=ts: s.dma_start(
                out=d["v0"].ap()[ts * 128:(ts + 1) * 128, :], in_=TBF[:, 0:D]))

        S.run()



def emit_gather(nc, d, g, gval, layer):
    sfx = str(layer)
    S = Ser(nc, g, gval)
    S.dma("gpsimd", lambda s: s.collective_compute(
        "AllGather", ALU.bypass,
        ins=[d["kT" + sfx].ap()], outs=[d["kTg" + sfx].ap()],
        replica_groups=RG))
    S.dma("gpsimd", lambda s: s.collective_compute(
        "AllGather", ALU.bypass,
        ins=[d["v" + sfx].ap()], outs=[d["vg" + sfx].ap()],
        replica_groups=RG))
    with ExitStack() as ctx:
        S.run()



def emit_attn(nc, d, sems, semval, causal):
    L = "1" if causal else "0"
    pfx = "a" + L + "_"
    with ExitStack() as ctx:
        QT = _sb(nc, ctx, pfx + "qt", [64, 2 * T], BF16)
        KTh = _sb(nc, ctx, pfx + "kth", [64, 2 * SEQ], BF16)
        VH = _sb(nc, ctx, pfx + "vh", [128, 2 * NKB * 64], BF16)
        VE = _sb(nc, ctx, pfx + "ve", [128, 2 * 65], BF16)
        PTt = _sb(nc, ctx, pfx + "ptt", [128, T], BF16)
        YT = _sb(nc, ctx, pfx + "yt", [128, 8 * T], BF16)
        TMPY = _sb(nc, ctx, pfx + "tmpy", [128, T], F32)
        DR = _sb(nc, ctx, pfx + "dr", [1, T], F32)
        ONES1 = _sb(nc, ctx, pfx + "ones1", [1, 64], F32)
        OW = _sb(nc, ctx, pfx + "ow", [128, 8 * D], BF16)
        OST = _sb(nc, ctx, pfx + "ost", [128, 2 * D], F32)
        if not causal:
            XNt = _sb(nc, ctx, pfx + "xnt", [128, 2 * D], F32)
            OB = _sb(nc, ctx, pfx + "ob", [128, D], F32)
        if causal:
            MT = _sb(nc, ctx, pfx + "mt", [128, 16 * 512], BF16)
        psS = _ps(nc, ctx, pfx + "pss", [128, 2048])
        psY = _ps(nc, ctx, pfx + "psy", [128, 2048])

        S = Sched(nc, sems, semval)
        qT_t = d["qT" + L]
        kTg = d["kTg" + L]
        vg = d["vg" + L]
        s_o = d["s" + L]
        oww = d["owT" + L]
        for c in range(8):
            S.dma("sync", lambda s, c=c: s.dma_start(
                out=OST[:, 0:D], in_=oww.ap()[c * 128:(c + 1) * 128, :]))
            S.op("vector", lambda v, c=c: v.tensor_copy(
                OW[:, c * D:(c + 1) * D], OST[:, 0:D]))
        S.op("vector", lambda v: v.memset(VE[:, 64:65], 1.0))
        S.op("vector", lambda v: v.memset(VE[:, 129:130], 1.0))
        S.op("vector", lambda v: v.memset(ONES1[:, :], 1.0))
        if not causal:
            S.dma("sync", lambda s: s.dma_start(out=OB[:, :], in_=d["outb_r"].ap()))
        else:
            mtl = S.dma("sync", lambda s: s.dma_start(out=MT[:, :], in_=d["mkt"].ap()))

        vg_r = vg.ap().rearrange("(n p) d -> p n d", p=128)
        VH_r = VH.ap().rearrange("p (n d) -> p n d", d=64)
        escale = 8.0 if not causal else 1.0 / 64.0

        ktl = [None] * 16
        qtl = [None] * 16
        vhl = [[None] * 8 for _ in range(16)]

        def load_head(h, deps):
            b = h % 2
            ktl[h] = S.dma("sync", lambda s, h=h, b=b: s.dma_start(
                out=KTh[0:64, b * SEQ:(b + 1) * SEQ],
                in_=kTg.ap()[h * 64:(h + 1) * 64, :]), deps=deps)
            qtl[h] = S.dma("sync", lambda s, h=h, b=b: s.dma_start(
                out=QT[0:64, b * T:(b + 1) * T],
                in_=qT_t.ap()[h * 64:(h + 1) * 64, :]), deps=deps)
            for j in range(8):
                vhl[h][j] = S.dma("sync", lambda s, h=h, j=j, b=b: s.dma_start(
                    out=VH_r[:, b * NKB + j * 8: b * NKB + (j + 1) * 8, :],
                    in_=vg_r[:, j * 8:(j + 1) * 8, h * 64:(h + 1) * 64]), deps=deps)

        load_head(0, None)
        load_head(1, [])
        prev_recip = None
        prev_tmpy = None
        prev_ytmul = None
        lastS = [None] * 16
        lastVE = [None] * 16
        ytmuls = []
        exp_hist = {}
        pv_hist = {}
        mul_hist = {}
        for h in range(16):
            hb = h % 2
            hp = (h % 2) * 64
            ve_hist = {}
            exp_hist.clear(); pv_hist.clear(); mul_hist.clear()
            pv_of_kb = {}
            if causal:
                def emit_pv_c(a):
                    kb, k, ib, jb, gate, veh, jj = a
                    pv_deps = [gate, veh]
                    if jj == 0 and prev_tmpy is not None:
                        pv_deps += [prev_tmpy, prev_recip]
                    return S.op("tensor", lambda t, kb=kb, k=k, ib=ib, jb=jb: t.matmul(
                        psY[0:65, k * 512:(k + 1) * 512],
                        VE[:, ib * 65: ib * 65 + 65],
                        PTt[:, jb * 512:(jb + 1) * 512],
                        start=(kb == 0), stop=(kb == 16 * k + 15),
                        skip_group_check=True), deps=pv_deps)

                j = -1
                pend = None
                for kb in range(NKB):
                    ib = kb % 2
                    ve_deps = [vhl[h][kb // 8]]
                    if kb - 2 >= 0:
                        ve_deps.append(pv_hist[pv_of_kb[kb - 2]])
                    elif prev_ytmul is not None:
                        ve_deps.append(prev_ytmul)
                    ve_hist[kb] = S.op("vector", lambda v, kb=kb, ib=ib, hb=hb: v.tensor_copy(
                        VE[:, ib * 65: ib * 65 + 64],
                        VH[:, hb * NKB * 64 + kb * 64: hb * NKB * 64 + (kb + 1) * 64]),
                        deps=ve_deps)
                    for k in range(kb // 16, 4):
                        j += 1
                        jb = j % 2
                        s_deps = [ktl[h], qtl[h]]
                        if j - 2 in exp_hist:
                            s_deps.append(exp_hist[j - 2])
                        elif prev_ytmul is not None:
                            s_deps.append(prev_ytmul)
                        slast = S.op("tensor", lambda t, kb=kb, k=k, jb=jb, hb=hb: t.matmul(
                            psS[:, jb * 512:(jb + 1) * 512],
                            KTh[0:64, hb * SEQ + kb * 128: hb * SEQ + (kb + 1) * 128],
                            QT[0:64, hb * T + k * 512:hb * T + (k + 1) * 512],
                            start=True, stop=True), deps=s_deps)
                        lastS[h] = slast
                        e_deps = [slast]
                        if j - 2 in pv_hist:
                            e_deps.append(pv_hist[j - 2])
                        elif prev_ytmul is not None:
                            e_deps.append(prev_ytmul)
                        ex = S.op("scalar", lambda a, jb=jb: a.activation(
                            PTt[:, jb * 512:(jb + 1) * 512],
                            psS[:, jb * 512:(jb + 1) * 512], AF.Exp, scale=escale),
                            deps=e_deps)
                        exp_hist[j] = ex
                        gate = ex
                        t_ = kb - 16 * k
                        if t_ >= 0:
                            gate = S.op("vector", lambda v, jb=jb, t_=t_: v.tensor_mul(
                                PTt[:, jb * 512:(jb + 1) * 512],
                                PTt[:, jb * 512:(jb + 1) * 512],
                                MT[:, t_ * 512:(t_ + 1) * 512]), deps=[ex, mtl])
                        if pend is not None:
                            pv_hist[pend[6]] = emit_pv_c(pend)
                        pend = (kb, k, ib, jb, gate, ve_hist[kb], j)
                    pv_of_kb[kb] = j
                pv_hist[pend[6]] = emit_pv_c(pend)
                pend = None
                jlast = j
            else:
                def emit_pv_f(a):
                    kb, qh, ib, jb, gate, veh = a
                    pv_deps = [gate, veh]
                    if kb == 0 and prev_tmpy is not None:
                        pv_deps += [prev_tmpy, prev_recip]
                    pvlast = None
                    for i2 in range(2):
                        pvlast = S.op("tensor", lambda t, kb=kb, qh=qh, i2=i2, ib=ib, jb=jb: t.matmul(
                            psY[0:65, qh * 1024 + i2 * 512: qh * 1024 + (i2 + 1) * 512],
                            VE[:, ib * 65: ib * 65 + 65],
                            PTt[:, jb * 1024 + i2 * 512: jb * 1024 + (i2 + 1) * 512],
                            start=(kb == 0), stop=(kb == NKB - 1),
                            skip_group_check=True), deps=list(pv_deps))
                    return pvlast

                pend = None
                for kb in range(NKB):
                    ib = kb % 2
                    ve_deps = [vhl[h][kb // 8]]
                    if kb - 2 >= 0:
                        ve_deps.append(pv_hist[pv_of_kb[kb - 2]])
                    elif kb == 0 and prev_ytmul is not None:
                        ve_deps.append(prev_ytmul)
                    ve_hist[kb] = S.op("vector", lambda v, kb=kb, ib=ib, hb=hb: v.tensor_copy(
                        VE[:, ib * 65: ib * 65 + 64],
                        VH[:, hb * NKB * 64 + kb * 64: hb * NKB * 64 + (kb + 1) * 64]),
                        deps=ve_deps)
                    for qh in range(2):
                        j = kb * 2 + qh
                        jb = j % 2
                        s_deps = [ktl[h], qtl[h]]
                        if j - 2 in exp_hist:
                            s_deps.append(exp_hist[j - 2])
                        if kb == 0 and prev_ytmul is not None:
                            s_deps.append(prev_ytmul)
                        slast = None
                        for i2 in range(2):
                            slast = S.op("tensor", lambda t, kb=kb, qh=qh, i2=i2, jb=jb, hb=hb: t.matmul(
                                psS[:, jb * 1024 + i2 * 512: jb * 1024 + (i2 + 1) * 512],
                                KTh[0:64, hb * SEQ + kb * 128: hb * SEQ + (kb + 1) * 128],
                                QT[0:64, hb * T + qh * 1024 + i2 * 512: hb * T + qh * 1024 + (i2 + 1) * 512],
                                start=True, stop=True), deps=list(s_deps))
                        lastS[h] = slast
                        e_deps = [slast]
                        if j - 2 in pv_hist:
                            e_deps.append(pv_hist[j - 2])
                        elif kb == 0 and prev_ytmul is not None:
                            e_deps.append(prev_ytmul)
                        ex = S.op("scalar", lambda a, jb=jb: a.activation(
                            PTt[:, jb * 1024:(jb + 1) * 1024],
                            psS[:, jb * 1024:(jb + 1) * 1024], AF.Exp, scale=escale),
                            deps=e_deps)
                        exp_hist[j] = ex
                        if pend is not None:
                            pv_hist[pend[1] + 2 * pend[0]] = emit_pv_f(pend)
                        pend = (kb, qh, ib, jb, ex, ve_hist[kb])
                    pv_of_kb[kb] = kb * 2 + 1
                pv_hist[pend[1] + 2 * pend[0]] = emit_pv_f(pend)
                pend = None
                jlast = NKB * 2 - 1
            lastVE[h] = ve_hist[NKB - 1]
            rc = S.op("vector", lambda v: v.reciprocal(DR[:, :], psY[64:65, :]),
                      deps=[pv_hist[jlast]])
            bc = None
            for j in range(4):
                bc = S.op("tensor", lambda t, j=j: t.matmul(
                    psS[0:64, j * 512:(j + 1) * 512],
                    ONES1[:, :], DR[0:1, j * 512:(j + 1) * 512],
                    start=True, stop=True),
                    deps=[rc, exp_hist[jlast], exp_hist[jlast - 1]])
            tc = S.op("vector", lambda v: v.tensor_copy(TMPY[0:64, :], psY[0:64, :]),
                      deps=[pv_hist[jlast]])
            ym = S.op("vector", lambda v, hp=hp, h=h: v.tensor_mul(
                YT[hp:hp + 64, (h // 2) * T:(h // 2 + 1) * T],
                TMPY[0:64, :], psS[0:64, :]), deps=[bc, tc])
            ytmuls.append(ym)
            prev_recip = rc
            prev_tmpy = tc
            prev_ytmul = ym
            if h + 2 < 16:
                load_head(h + 2, [lastS[h], lastVE[h]])

        fin_hist = {}
        xnl_hist = {}
        st_hist = {}
        for ts in range(NTS):
            tb = ts % 2
            mm_deps = list(ytmuls)
            if ts == 0:
                mm_deps += [prev_tmpy, prev_recip]
            if ts - 2 >= 0:
                mm_deps.append(fin_hist[ts - 2])
            mmlast = None
            for half in range(2):
                for c in range(8):
                    mmlast = S.op("tensor", lambda t, ts=ts, c=c, half=half, tb=tb: t.matmul(
                        psY[:, tb * 1024 + half * 512: tb * 1024 + (half + 1) * 512],
                        YT[:, c * T + ts * 128: c * T + ts * 128 + 128],
                        OW[:, c * D + half * 512: c * D + (half + 1) * 512],
                        start=(c == 0), stop=(c == 7)), deps=mm_deps)
            if not causal:
                xd = [st_hist[ts - 2]] if ts - 2 >= 0 else []
                xnl = S.dma("sync", lambda s, ts=ts, tb=tb: s.dma_start(
                    out=XNt[:, tb * D:(tb + 1) * D],
                    in_=d["xn0"].ap()[ts * 128:(ts + 1) * 128, :]), deps=xd)
                a1 = S.op("vector", lambda v, tb=tb: v.tensor_add(
                    OST[:, tb * D:(tb + 1) * D], psY[:, tb * 1024: tb * 1024 + 1024],
                    OB[:, :]), deps=[mmlast] + xd)
                fin = S.op("vector", lambda v, tb=tb: v.tensor_add(
                    OST[:, tb * D:(tb + 1) * D], OST[:, tb * D:(tb + 1) * D],
                    XNt[:, tb * D:(tb + 1) * D]), deps=[a1, xnl])
            else:
                sd = [st_hist[ts - 2]] if ts - 2 >= 0 else []
                fin = S.op("scalar", lambda a, tb=tb: a.activation(
                    OST[:, tb * D:(tb + 1) * D], psY[:, tb * 1024: tb * 1024 + 1024],
                    AF.Copy, scale=0.25), deps=[mmlast] + sd)
            fin_hist[ts] = fin
            st_hist[ts] = S.dma("gpsimd", lambda s, ts=ts, tb=tb: s.dma_start(
                out=s_o.ap()[ts * 128:(ts + 1) * 128, :],
                in_=OST[:, tb * D:(tb + 1) * D]), deps=[fin])
        S.last = st_hist[NTS - 1]

        S.run()



def emit_ffn0(nc, d, sems, semval):
    with ExitStack() as ctx:
        CONSTS = _sb(nc, ctx, "f0_consts", [128, 4], F32)
        XT = _sb(nc, ctx, "f0_xt", [128, 8 * T], BF16)
        YAC = _sb(nc, ctx, "f0_yac", [128, 8 * T], F32)
        WST = _sb(nc, ctx, "f0_wst", [128, 2048], F32)
        WG = _sb(nc, ctx, "f0_wg", [128, 2 * 4096], BF16)
        WU = _sb(nc, ctx, "f0_wu", [128, 2 * 4096], BF16)
        WD = _sb(nc, ctx, "f0_wd", [128, 2 * 4096], BF16)
        INT = _sb(nc, ctx, "f0_int", [128, 4096], BF16)
        SG = _sb(nc, ctx, "f0_sg", [128, 1024], F32)
        SCR = _sb(nc, ctx, "f0_scr", [128, 2048], F32)
        U2 = _sb(nc, ctx, "f0_u2", [128, 2 * D], F32)
        XF = U2[:, D:2 * D]
        XDB = _sb(nc, ctx, "f0_xdb", [128, 2 * D], F32)
        GWT = _sb(nc, ctx, "f0_gwt", [128, 8 * N_EXP], F32)
        EB = _sb(nc, ctx, "f0_eb", [128, N_EXP], F32)
        IDENT = _sb(nc, ctx, "f0_ident", [128, 128], F32)
        WT = _sb(nc, ctx, "f0_wt16", [16, T], F32)
        WROW = U2[0:1, 0:T]
        WB = _sb(nc, ctx, "f0_wb", [128, T], F32)
        ONESR = _sb(nc, ctx, "f0_onesr", [1, 128], F32)
        C14 = _sb(nc, ctx, "f0_c14", [128, 4], F32)
        GS = _sb(nc, ctx, "f0_gs", [128, 128], F32)
        CNTE16 = _sb(nc, ctx, "f0_cnte", [128, 16], F32)
        SEL = _sb(nc, ctx, "f0_sel", [128, 16], F32)
        WGT = _sb(nc, ctx, "f0_wgt", [128, 16], F32)
        SWc = _sb(nc, ctx, "f0_swc", [128, 2], F32)
        psG = _ps(nc, ctx, "f0_psg", [128, 1024])
        psU = _ps(nc, ctx, "f0_psu", [128, 1024])
        psD = _ps(nc, ctx, "f0_psd", [128, 1024])
        psT = _ps(nc, ctx, "f0_pst", [128, 512])
        psW = _ps(nc, ctx, "f0_psw", [128, 16])

        SCO = GS[:, 0:16]
        SFC = GS[:, 16:32]
        P6 = GS[:, 32:56]
        T2 = GS[:, 56:60]
        CNT = GS[:, 60:64]
        GM = GS[:, 64:68]
        SM = GS[:, 68:84]
        TMPS = GS[:, 84:100]
        S16 = GS[:, 100:116]
        E1 = GS[:, 116:117]

        S = Sched(nc, sems, semval)
        S.dma("sync", lambda s: s.dma_start(out=IDENT[:, :], in_=d["c_ident"].ap()))
        S.dma("sync", lambda s: s.dma_start(out=CONSTS[:, :], in_=d["c_consts4"].ap()))
        S.dma("sync", lambda s: s.dma_start(out=EB[:, :], in_=d["ebias_r"].ap()))
        for c in range(8):
            S.dma("sync", lambda s, c=c: s.dma_start(
                out=GWT[:, c * 16:(c + 1) * 16], in_=d["gate_wT"].ap()[c * 128:(c + 1) * 128, :]))
        S.op("vector", lambda v: v.memset(C14[:, :], 1.0))
        S.op("vector", lambda v: v.memset(ONESR[:, :], 1.0))

        def ap_str4(off):
            return GS[:, 16 + off: 32: 4]

        gx_load = {}
        gx_last = {}

        def issue_gx(ts, deps):
            cur = S.last
            gx_load[ts] = S.dma("sync", lambda s, ts=ts, p=ts % 2: s.dma_start(
                out=XDB[:, p * D:(p + 1) * D],
                in_=d["xn0"].ap()[ts * 128:(ts + 1) * 128, :]), deps=deps)
            S.last = cur

        issue_gx(0, [])
        issue_gx(1, [])
        for ts in range(NTS):
            p = ts % 2
            if 2 <= ts + 1 < NTS:
                issue_gx(ts + 1, [gx_last[ts - 1]])
            gxl = None
            for grp in range(2):
                for c in range(4):
                    tdeps = None
                    if grp == 0 and c == 0:
                        tdeps = ([S.last, gx_load[ts]]
                                 if S.last is not None else [gx_load[ts]])
                    gxl = S.op("tensor", lambda t, grp=grp, c=c, p=p: t.transpose(
                        psT[:, c * 128:(c + 1) * 128],
                        XDB[:, p * D + (grp * 4 + c) * 128: p * D + (grp * 4 + c + 1) * 128],
                        IDENT[:, :]), deps=tdeps)
                S.op("vector", lambda v, grp=grp: v.tensor_copy(
                    XF[:, grp * 512:(grp + 1) * 512], psT[:, :]))
                for c in range(4):
                    S.op("vector", lambda v, grp=grp, c=c, ts=ts: v.tensor_copy(
                        XT[:, (grp * 4 + c) * T + ts * 128: (grp * 4 + c) * T + ts * 128 + 128],
                        psT[:, c * 128:(c + 1) * 128]))
            gx_last[ts] = gxl
            for c in range(8):
                S.op("tensor", lambda t, c=c: t.matmul(
                    psW[:, :], XF[:, c * 128:(c + 1) * 128],
                    GWT[:, c * 16:(c + 1) * 16], start=(c == 0), stop=(c == 7)))
            S.op("scalar", lambda a: a.activation(SCO, psW[:, :], AF.Sigmoid))
            S.op("vector", lambda v: v.tensor_add(SFC, SCO, EB[:, :]))
            for k, (ii, jj) in enumerate(_PAIRS):
                S.op("vector", lambda v, k=k, ii=ii, jj=jj: v.tensor_add(
                    P6[:, k * 4:(k + 1) * 4], ap_str4(ii), ap_str4(jj)))
            S.op("vector", lambda v: v.tensor_copy(T2, P6[:, 0:4]))
            for k in range(1, 6):
                S.op("vector", lambda v, k=k: v.tensor_max(T2, T2, P6[:, k * 4:(k + 1) * 4]))
            for gI in range(4):
                S.op("vector", lambda v, gI=gI: v.tensor_single_scalar(
                    P6[:, 0:4], T2, T2[:, gI:gI + 1], ALU.is_gt))
                S.op("vector", lambda v, gI=gI: v.reduce_sum(CNT[:, gI:gI + 1], P6[:, 0:4], AX.X))
            for gI in range(1, 4):
                S.op("vector", lambda v, gI=gI: v.tensor_single_scalar(
                    P6[:, 0:gI], T2[:, 0:gI], T2[:, gI:gI + 1], ALU.is_equal))
                S.op("vector", lambda v, gI=gI: v.reduce_sum(E1, P6[:, 0:gI], AX.X))
                S.op("vector", lambda v, gI=gI: v.tensor_add(CNT[:, gI:gI + 1], CNT[:, gI:gI + 1], E1))
            S.op("vector", lambda v: v.tensor_single_scalar(GM, CNT, CONSTS[:, 0:1], ALU.is_le))
            for gI in range(4):
                S.op("vector", lambda v, gI=gI: v.tensor_single_scalar(
                    SM[:, gI * 4:(gI + 1) * 4], C14[:, :], GM[:, gI:gI + 1], ALU.mult))
            S.op("vector", lambda v: v.tensor_mul(TMPS, SFC, SM))
            for e in range(16):
                S.op("vector", lambda v, e=e: v.tensor_single_scalar(
                    S16, TMPS, TMPS[:, e:e + 1], ALU.is_gt))
                S.op("vector", lambda v, e=e: v.reduce_sum(CNTE16[:, e:e + 1], S16, AX.X))
            for e in range(1, 16):
                S.op("vector", lambda v, e=e: v.tensor_single_scalar(
                    S16[:, 0:e], TMPS[:, 0:e], TMPS[:, e:e + 1], ALU.is_equal))
                S.op("vector", lambda v, e=e: v.reduce_sum(E1, S16[:, 0:e], AX.X))
                S.op("vector", lambda v, e=e: v.tensor_add(CNTE16[:, e:e + 1], CNTE16[:, e:e + 1], E1))
            S.op("vector", lambda v: v.tensor_single_scalar(SEL[:, :], CNTE16[:, :], CONSTS[:, 1:2], ALU.is_le))
            S.op("vector", lambda v: v.tensor_mul(WGT[:, :], SCO, SEL[:, :]))
            S.op("vector", lambda v: v.reduce_sum(SWc[:, 0:1], WGT[:, :], AX.X))
            S.op("vector", lambda v: v.tensor_single_scalar(SWc[:, 0:1], SWc[:, 0:1], CONSTS[:, 2:3], ALU.add))
            S.op("vector", lambda v: v.reciprocal(SWc[:, 1:2], SWc[:, 0:1]))
            S.op("vector", lambda v: v.tensor_scalar(WGT[:, :], WGT[:, :], SWc[:, 1:2], CONSTS[:, 3:4], ALU.mult, ALU.mult))
            S.op("tensor", lambda t: t.transpose(psT[0:16, 0:128], WGT[:, :], IDENT[:, :]))
            S.op("vector", lambda v, ts=ts: v.tensor_copy(
                WT[0:16, ts * 128:(ts + 1) * 128], psT[0:16, 0:128]))

        gate_end = S.last

        st = {"lastG": [None, None], "lastU": [None, None],
              "lastD": [None, None], "silu": None, "mul": None,
              "down_of_int": [None, None], "scradd": {}, "scradd_oc": {},
              "scrmul_last": None,
              "bcast_last": gate_end,
              "wstp": 0, "wst_cast": [gate_end, gate_end]}
        castG = [None, None]
        castU = [None, None]
        castD = [None, None]
        _WQUEUE = {"g": "sync", "u": "scalar", "d": "gpsimd"}

        def load_w(which, src_fn, dst, dst_off, reader, e_key):
            cast = None
            nsub = 2 if which != "d" else 1
            for chunk in range(4):
                p = st["wstp"] % 2
                st["wstp"] += 1
                dma = S.dma(_WQUEUE[which], lambda s, chunk=chunk, p=p, nsub=nsub: s.dma_start(
                    out=WST.ap()[:, p * 1024:(p + 1) * 1024].rearrange(
                        "p (c i) -> p c i", c=nsub),
                    in_=src_fn(chunk)), deps=[st["wst_cast"][p]])
                cast = S.op("gpsimd", lambda g, chunk=chunk, p=p, dst=dst, dst_off=dst_off: g.tensor_copy(
                    dst[:, dst_off + chunk * 1024: dst_off + (chunk + 1) * 1024],
                    WST[:, p * 1024:(p + 1) * 1024]),
                    deps=[dma] + ([reader] if reader is not None else []))
                st["wst_cast"][p] = cast
            return cast

        def load_expert(e):
            b = e % 2
            gp_src = lambda chunk: d["gpT"].ap()[e:e + 1, chunk * 256:(chunk + 1) * 256, :].rearrange(
                "e (c p) i -> p (e c) i", p=128)
            up_src = lambda chunk: d["upT"].ap()[e:e + 1, chunk * 256:(chunk + 1) * 256, :].rearrange(
                "e (c p) i -> p (e c) i", p=128)
            dp_src = lambda chunk: d["dpT"].ap()[e:e + 1, chunk * 128:(chunk + 1) * 128, :].rearrange(
                "e (c p) o -> p (e c) o", p=128)
            castG[b] = load_w("g", gp_src, WG, b * 4096, st["lastG"][b], e)
            castU[b] = load_w("u", up_src, WU, b * 4096, st["lastU"][b], e)
            castD[b] = load_w("d", dp_src, WD, b * 4096, st["lastD"][b], e)

        def load_shared():
            sg_src = lambda chunk: d["sgT"].ap()[chunk * 256:(chunk + 1) * 256, :].rearrange(
                "(c p) i -> p c i", p=128)
            su_src = lambda chunk: d["suT"].ap()[chunk * 256:(chunk + 1) * 256, :].rearrange(
                "(c p) i -> p c i", p=128)
            sd_src = lambda chunk: d["sdT"].ap()[chunk * 128:(chunk + 1) * 128, :].rearrange(
                "(c p) o -> p c o", p=128)
            castG[0] = load_w("g", sg_src, WG, 0, None, -1)
            castU[0] = load_w("u", su_src, WU, 0, None, -1)
            castD[0] = load_w("d", sd_src, WD, 0, None, -1)

        def ffn_core(e):
            b = 0 if e < 0 else e % 2
            lastG = None
            lastU = None
            for tb in range(2):
                for ic in range(4):
                    gdeps0 = [castG[b]] + ([st["silu"]] if st["silu"] else [])
                    for half in range(2):
                        for dc in range(8):
                            lastG = S.op("tensor", lambda t, tb=tb, ic=ic, dc=dc, half=half, b=b: t.matmul(
                                psG[:, half * 512:(half + 1) * 512],
                                WG[:, b * 4096 + dc * 512 + ic * 128: b * 4096 + dc * 512 + (ic + 1) * 128],
                                XT[:, dc * T + tb * 1024 + half * 512: dc * T + tb * 1024 + (half + 1) * 512],
                                start=(dc == 0), stop=(dc == 7)), deps=gdeps0)
                    udeps0 = [castU[b]] + ([st["mul"]] if st["mul"] else [])
                    for half in range(2):
                        for dc in range(8):
                            lastU = S.op("tensor", lambda t, tb=tb, ic=ic, dc=dc, half=half, b=b: t.matmul(
                                psU[:, half * 512:(half + 1) * 512],
                                WU[:, b * 4096 + dc * 512 + ic * 128: b * 4096 + dc * 512 + (ic + 1) * 128],
                                XT[:, dc * T + tb * 1024 + half * 512: dc * T + tb * 1024 + (half + 1) * 512],
                                start=(dc == 0), stop=(dc == 7)), deps=udeps0)
                    sl = S.op("scalar", lambda a: a.activation(SG[:, :], psG[:, :], AF.Silu),
                              deps=[lastG] + ([st["mul"]] if st["mul"] else []))
                    st["silu"] = sl
                    mdeps = [sl, lastU]
                    if st["down_of_int"][ic % 2] is not None and tb == 1:
                        mdeps.append(st["down_of_int"][0])
                    ml = S.op("vector", lambda v, ic=ic: v.tensor_mul(
                        INT[:, ic * 1024:(ic + 1) * 1024], SG[:, :], psU[:, :]),
                        deps=mdeps + ([st["int_reader"]] if st.get("int_reader") else []))
                    st["mul"] = ml
                    st.setdefault("muls", {})[(tb, ic)] = ml
                last_down = None
                for oc in range(8):
                    for th in range(2):
                        ddeps = [castD[b]]
                        pk = (oc - 1, th)
                        if pk in st.setdefault("scrmuls", {}):
                            ddeps.append(st["scrmuls"][pk])
                        elif oc == 0 and st.get("prev_scr") is not None:
                            ddeps.append(st["prev_scr"])
                        for ic in range(4):
                            ddeps2 = ddeps + [st["muls"][(tb, ic)]]
                            last_down = S.op("tensor", lambda t, oc=oc, ic=ic, th=th, b=b: t.matmul(
                                psD[:, th * 512:(th + 1) * 512],
                                WD[:, b * 4096 + ic * 1024 + oc * 128: b * 4096 + ic * 1024 + (oc + 1) * 128],
                                INT[:, ic * 1024 + th * 512: ic * 1024 + (th + 1) * 512],
                                start=(ic == 0), stop=(ic == 3)), deps=ddeps2)
                        if e < 0:
                            sa = S.op("vector", lambda v, oc=oc, tb=tb, th=th: v.tensor_copy(
                                YAC[:, oc * T + tb * 1024 + th * 512: oc * T + tb * 1024 + (th + 1) * 512],
                                psD[:, th * 512:(th + 1) * 512]), deps=[last_down])
                            st["scrmuls"][(oc, th)] = sa
                            st["scradd"][(tb, oc, th)] = sa
                        else:
                            sq = (oc % 2) * 1024 + th * 512
                            slot = (oc % 2, th)
                            sm = S.op("vector", lambda v, tb=tb, th=th, sq=sq: v.tensor_mul(
                                SCR[:, sq:sq + 512], psD[:, th * 512:(th + 1) * 512],
                                WB[:, tb * 1024 + th * 512: tb * 1024 + (th + 1) * 512]),
                                deps=[last_down, st["wbready"]]
                                + ([st["scradd_oc"][slot]] if slot in st["scradd_oc"] else []))
                            st["scrmuls"][(oc, th)] = sm
                            sa = S.op("gpsimd", lambda g, oc=oc, tb=tb, th=th, sq=sq: g.tensor_add(
                                YAC[:, oc * T + tb * 1024 + th * 512: oc * T + tb * 1024 + (th + 1) * 512],
                                YAC[:, oc * T + tb * 1024 + th * 512: oc * T + tb * 1024 + (th + 1) * 512],
                                SCR[:, sq:sq + 512]),
                                deps=[sm, st["scradd"].get((tb, oc, th))])
                            st["scradd"][(tb, oc, th)] = sa
                            st["scradd_oc"][slot] = sa
                st["int_reader"] = last_down
            st["lastG"][b] = lastG
            st["lastU"][b] = lastU
            st["lastD"][b] = last_down
            st["prev_scr"] = st["scrmuls"][(7, 1)]
            st["scrmuls"].clear()

        load_shared()
        ffn_core(-1)
        load_expert(0)
        for e in range(N_EXP):
            wr = S.dma("gpsimd", lambda s, e=e: s.dma_start(
                out=WROW[:, :], in_=WT[e:e + 1, :]),
                deps=[st["bcast_last"], gate_end])
            bc = None
            prev_cp = None
            for j in range(4):
                bdeps = [wr] + ([st["prev_scr"]] if st.get("prev_scr") else [])
                if prev_cp is not None:
                    bdeps.append(prev_cp)
                bc = S.op("tensor", lambda t, j=j: t.matmul(
                    psT[:, :], ONESR[0:1, 0:128], WROW[0:1, j * 512:(j + 1) * 512],
                    start=True, stop=True), deps=bdeps)
                prev_cp = S.op("vector", lambda v, j=j: v.tensor_copy(
                    WB[:, j * 512:(j + 1) * 512], psT[:, :]), deps=[bc])
            st["wbready"] = S.last
            st["bcast_last"] = S.last
            if e + 1 < N_EXP:
                load_expert(e + 1)
            ffn_core(e)

        for ts in range(NTS):
            for grp in range(2):
                for c in range(4):
                    S.op("tensor", lambda t, ts=ts, grp=grp, c=c: t.transpose(
                        psT[:, c * 128:(c + 1) * 128],
                        YAC[:, (grp * 4 + c) * T + ts * 128: (grp * 4 + c) * T + ts * 128 + 128],
                        IDENT[:, :]))
                S.op("vector", lambda v, grp=grp: v.tensor_copy(
                    SCR[:, grp * 512:(grp + 1) * 512], psT[:, :]))
            S.dma("gpsimd", lambda s, ts=ts: s.dma_start(
                out=d["fy"].ap()[ts * 128:(ts + 1) * 128, :], in_=SCR[:, 0:1024]))

        S.run()



def emit_comb(nc, d, sems, semval):
    with ExitStack() as ctx:
        CONSTS = _sb(nc, ctx, "cb_consts", [128, 1], F32)
        WQ = _sb(nc, ctx, "cb_wq", [128, 8 * D], BF16)
        WK = _sb(nc, ctx, "cb_wk", [128, 8 * D], BF16)
        WV = _sb(nc, ctx, "cb_wv", [128, 8 * D], BF16)
        WSTG = _sb(nc, ctx, "cb_wstg", [128, D], F32)
        X = _sb(nc, ctx, "cb_x", [128, 2 * D], F32)
        Sv = _sb(nc, ctx, "cb_s", [128, 2 * D], F32)
        FY = _sb(nc, ctx, "cb_fy", [128, 2 * D], F32)
        G = _sb(nc, ctx, "cb_g", [128, D], F32)
        X1 = _sb(nc, ctx, "cb_x1", [128, D], F32)
        TMP = _sb(nc, ctx, "cb_tmp", [128, D], F32)
        XN1 = _sb(nc, ctx, "cb_xn1", [128, D], F32)
        XTb = _sb(nc, ctx, "cb_xtb", [128, D], BF16)
        QF = _sb(nc, ctx, "cb_qf", [128, D], F32)
        TBF = _sb(nc, ctx, "cb_tbf", [128, 6 * D], BF16)
        COLS = _sb(nc, ctx, "cb_cols", [128, 8], F32)
        IDENT = _sb(nc, ctx, "cb_ident", [128, 128], F32)
        psQ = _ps(nc, ctx, "cb_psq", [128, 1024])
        psK = _ps(nc, ctx, "cb_psk", [128, 1024])
        psV = _ps(nc, ctx, "cb_psv", [128, 1024])
        psT = _ps(nc, ctx, "cb_pst", [128, 512])

        S = Sched(nc, sems, semval)
        S.dma("sync", lambda s: s.dma_start(out=IDENT[:, :], in_=d["c_ident"].ap()))
        S.dma("sync", lambda s: s.dma_start(out=CONSTS[:, :], in_=d["c_eps"].ap()))
        for w_name, WB_ in (("wq", WQ), ("wk", WK), ("wv", WV)):
            for c in range(8):
                S.dma("sync", lambda s, w_name=w_name, c=c: s.dma_start(
                    out=WSTG[:, :], in_=d[w_name].ap()[c * 128:(c + 1) * 128, :]))
                S.op("vector", lambda v, WB_=WB_, c=c: v.tensor_copy(
                    WB_[:, c * D:(c + 1) * D], WSTG[:, :]))

        qT_r = d["qT1"].ap().rearrange("(c p) t -> p c t", p=128)
        kT_r = d["kT1"].ap().rearrange("(c p) t -> p c t", p=128)

        loads = {}
        cons = {}
        st3 = {}

        def issue_loads(ts, deps):
            p = ts % 2
            cur = S.last
            l1 = S.dma("sync", lambda s, ts=ts, p=p: s.dma_start(
                out=X[:, p * D:(p + 1) * D],
                in_=d["x"].ap()[ts * 128:(ts + 1) * 128, :]), deps=deps)
            l2 = S.dma("sync", lambda s, ts=ts, p=p: s.dma_start(
                out=Sv[:, p * D:(p + 1) * D],
                in_=d["s0"].ap()[ts * 128:(ts + 1) * 128, :]), deps=[])
            l3 = S.dma("sync", lambda s, ts=ts, p=p: s.dma_start(
                out=FY[:, p * D:(p + 1) * D],
                in_=d["fy"].ap()[ts * 128:(ts + 1) * 128, :]), deps=[])
            loads[ts] = (l1, l2, l3)
            S.last = cur

        issue_loads(0, [])
        issue_loads(1, [])
        for ts in range(NTS):
            p = ts % 2
            if 2 <= ts + 1 < NTS:
                issue_loads(ts + 1, list(cons[ts - 1]))
            sil = S.op("scalar", lambda a, p=p: a.activation(
                G[:, :], FY[:, p * D:(p + 1) * D], AF.Silu),
                deps=[S.last, loads[ts][2]] if S.last is not None else [loads[ts][2]])
            S.op("vector", lambda v, p=p: v.tensor_mul(
                TMP[:, :], Sv[:, p * D:(p + 1) * D], G[:, :]),
                deps=[sil, loads[ts][1]])
            a1 = S.op("vector", lambda v, p=p: v.tensor_add(
                X1[:, :], X[:, p * D:(p + 1) * D], TMP[:, :]),
                deps=[S.last, loads[ts][0]])
            cons[ts] = (a1, sil)
            S.dma("gpsimd", lambda s, ts=ts: s.dma_start(
                out=d["x1"].ap()[ts * 128:(ts + 1) * 128, :], in_=X1[:, :]))
            _ln(S, X1, TMP, XN1, G, COLS, 0, CONSTS[:, 0:1])
            S.dma("gpsimd", lambda s, ts=ts: s.dma_start(
                out=d["xn1"].ap()[ts * 128:(ts + 1) * 128, :], in_=XN1[:, :]))
            for grp in range(2):
                for c in range(4):
                    S.op("tensor", lambda t, grp=grp, c=c: t.transpose(
                        psT[:, c * 128:(c + 1) * 128],
                        XN1[:, (grp * 4 + c) * 128:(grp * 4 + c + 1) * 128],
                        IDENT[:, :]))
                S.op("vector", lambda v, grp=grp: v.tensor_copy(
                    XTb[:, grp * 512:(grp + 1) * 512], psT[:, :]))
            for WB_, psX in ((WQ, psQ), (WK, psK), (WV, psV)):
                for half in range(2):
                    for c in range(8):
                        S.op("tensor", lambda t, WB_=WB_, psX=psX, c=c, half=half: t.matmul(
                            psX[:, half * 512:(half + 1) * 512],
                            XTb[:, c * 128:(c + 1) * 128],
                            WB_[:, c * D + half * 512: c * D + (half + 1) * 512],
                            start=(c == 0), stop=(c == 7)))
            for idx, (psX, name_r) in enumerate(((psQ, qT_r), (psK, kT_r))):
                reg = (3 * p + idx) * D
                S.op("scalar", lambda a, psX=psX: a.activation(QF[:, :], psX[:, :], AF.Copy, scale=0.25))
                cpy = None
                for grp in range(2):
                    for c in range(4):
                        S.op("tensor", lambda t, grp=grp, c=c: t.transpose(
                            psT[:, c * 128:(c + 1) * 128],
                            QF[:, (grp * 4 + c) * 128:(grp * 4 + c + 1) * 128],
                            IDENT[:, :]))
                    cdeps = None
                    if grp == 0 and (p, idx) in st3:
                        cdeps = [S.last, st3[(p, idx)]]
                    cpy = S.op("vector", lambda v, grp=grp, reg=reg: v.tensor_copy(
                        TBF[:, reg + grp * 512: reg + (grp + 1) * 512], psT[:, :]),
                        deps=cdeps)
                cur = S.last
                st3[(p, idx)] = S.dma("gpsimd", lambda s, name_r=name_r, ts=ts, reg=reg: s.dma_start(
                    out=name_r[:, :, ts * 128:(ts + 1) * 128],
                    in_=TBF.ap()[:, reg:reg + D].rearrange("p (c t) -> p c t", c=8)),
                    deps=[cpy])
                S.last = cur
            regv = (3 * p + 2) * D
            vdeps = [S.last] + ([st3[(p, 2)]] if (p, 2) in st3 else [])
            vc = S.op("scalar", lambda a, regv=regv: a.activation(
                TBF[:, regv:regv + D], psV[:, :], AF.Copy, scale=0.25), deps=vdeps)
            cur = S.last
            st3[(p, 2)] = S.dma("gpsimd", lambda s, ts=ts, regv=regv: s.dma_start(
                out=d["v1"].ap()[ts * 128:(ts + 1) * 128, :],
                in_=TBF[:, regv:regv + D]), deps=[vc])
            S.last = cur

        S.run()



def emit_p5(nc, d, sems, semval):
    with ExitStack() as ctx:
        W1B = _sb(nc, ctx, "p5_w1b", [128, 8 * L1H], BF16)
        W2B = _sb(nc, ctx, "p5_w2b", [128, 16 * D], BF16)
        WSTG = _sb(nc, ctx, "p5_wstg", [128, L1H], F32)
        B1 = _sb(nc, ctx, "p5_b1", [128, L1H], F32)
        B2 = _sb(nc, ctx, "p5_b2", [128, D], F32)
        XN1 = _sb(nc, ctx, "p5_xn1", [128, 2 * D], F32)
        XTb = _sb(nc, ctx, "p5_xtb", [128, D], BF16)
        HR = _sb(nc, ctx, "p5_hr", [128, L1H], F32)
        HT = _sb(nc, ctx, "p5_ht", [128, L1H], BF16)
        OY = _sb(nc, ctx, "p5_oy", [128, D], F32)
        X1 = _sb(nc, ctx, "p5_x1", [128, 2 * D], F32)
        S1 = _sb(nc, ctx, "p5_s1", [128, 2 * D], F32)
        IDENT = _sb(nc, ctx, "p5_ident", [128, 128], F32)
        psH = _ps(nc, ctx, "p5_psh", [128, 2048])
        psO = _ps(nc, ctx, "p5_pso", [128, 1024])
        psT = _ps(nc, ctx, "p5_pst", [128, 512])

        S = Sched(nc, sems, semval)
        S.dma("sync", lambda s: s.dma_start(out=IDENT[:, :], in_=d["c_ident"].ap()))
        S.dma("sync", lambda s: s.dma_start(out=B1[:, :], in_=d["b1_r"].ap()))
        S.dma("sync", lambda s: s.dma_start(out=B2[:, :], in_=d["b2_r"].ap()))
        for c in range(8):
            S.dma("sync", lambda s, c=c: s.dma_start(
                out=WSTG[:, :], in_=d["w1"].ap()[c * 128:(c + 1) * 128, :]))
            S.op("vector", lambda v, c=c: v.tensor_copy(
                W1B[:, c * L1H:(c + 1) * L1H], WSTG[:, :]))
        for c in range(16):
            S.dma("sync", lambda s, c=c: s.dma_start(
                out=WSTG[:, 0:D], in_=d["w2"].ap()[c * 128:(c + 1) * 128, :]))
            S.op("vector", lambda v, c=c: v.tensor_copy(
                W2B[:, c * D:(c + 1) * D], WSTG[:, 0:D]))

        loads5 = {}
        tr_last = {}
        oy_done = {}

        def issue_loads5(ts, dx, dxs):
            p = ts % 2
            cur = S.last
            ln_ = S.dma("sync", lambda s, ts=ts, p=p: s.dma_start(
                out=XN1[:, p * D:(p + 1) * D],
                in_=d["xn1"].ap()[ts * 128:(ts + 1) * 128, :]), deps=dx)
            lx = S.dma("sync", lambda s, ts=ts, p=p: s.dma_start(
                out=X1[:, p * D:(p + 1) * D],
                in_=d["x1"].ap()[ts * 128:(ts + 1) * 128, :]), deps=dxs)
            ls = S.dma("sync", lambda s, ts=ts, p=p: s.dma_start(
                out=S1[:, p * D:(p + 1) * D],
                in_=d["s1"].ap()[ts * 128:(ts + 1) * 128, :]), deps=[])
            loads5[ts] = (ln_, lx, ls)
            S.last = cur

        issue_loads5(0, [], [])
        issue_loads5(1, [], [])
        for ts in range(NTS):
            p = ts % 2
            if 2 <= ts + 1 < NTS:
                issue_loads5(ts + 1, [tr_last[ts - 1]], [oy_done[ts - 1]])
            trl = None
            for grp in range(2):
                for c in range(4):
                    tdeps = None
                    if grp == 0 and c == 0:
                        tdeps = ([S.last, loads5[ts][0]]
                                 if S.last is not None else [loads5[ts][0]])
                    trl = S.op("tensor", lambda t, grp=grp, c=c, p=p: t.transpose(
                        psT[:, c * 128:(c + 1) * 128],
                        XN1[:, p * D + (grp * 4 + c) * 128: p * D + (grp * 4 + c + 1) * 128],
                        IDENT[:, :]), deps=tdeps)
                S.op("vector", lambda v, grp=grp: v.tensor_copy(
                    XTb[:, grp * 512:(grp + 1) * 512], psT[:, :]))
            tr_last[ts] = trl
            for nb in range(4):
                for c in range(8):
                    S.op("tensor", lambda t, nb=nb, c=c: t.matmul(
                        psH[:, nb * 512:(nb + 1) * 512],
                        XTb[:, c * 128:(c + 1) * 128],
                        W1B[:, c * L1H + nb * 512: c * L1H + (nb + 1) * 512],
                        start=(c == 0), stop=(c == 7)))
            for nb in range(2):
                S.op("vector", lambda v, nb=nb: v.tensor_add(
                    HR[:, nb * 1024:(nb + 1) * 1024],
                    psH[:, nb * 1024:(nb + 1) * 1024],
                    B1[:, nb * 1024:(nb + 1) * 1024]))
            S.op("scalar", lambda a: a.activation(HR[:, :], HR[:, :], AF.Relu))
            for grp in range(4):
                for c in range(4):
                    S.op("tensor", lambda t, grp=grp, c=c: t.transpose(
                        psT[:, c * 128:(c + 1) * 128],
                        HR[:, (grp * 4 + c) * 128:(grp * 4 + c + 1) * 128],
                        IDENT[:, :]))
                S.op("vector", lambda v, grp=grp: v.tensor_copy(
                    HT[:, grp * 512:(grp + 1) * 512], psT[:, :]))
            for half in range(2):
                for c in range(16):
                    S.op("tensor", lambda t, c=c, half=half: t.matmul(
                        psO[:, half * 512:(half + 1) * 512],
                        HT[:, c * 128:(c + 1) * 128],
                        W2B[:, c * D + half * 512: c * D + (half + 1) * 512],
                        start=(c == 0), stop=(c == 15)))
            S.op("vector", lambda v: v.tensor_add(OY[:, :], psO[:, :], B2[:, :]))
            sil5 = S.op("scalar", lambda a: a.activation(OY[:, :], OY[:, :], AF.Silu))
            S.op("vector", lambda v, p=p: v.tensor_mul(
                OY[:, :], OY[:, :], S1[:, p * D:(p + 1) * D]),
                deps=[sil5, loads5[ts][2]])
            oya = S.op("vector", lambda v, p=p: v.tensor_add(
                OY[:, :], OY[:, :], X1[:, p * D:(p + 1) * D]),
                deps=[S.last, loads5[ts][1]])
            oy_done[ts] = oya
            S.dma("gpsimd", lambda s, ts=ts: s.dma_start(
                out=d["out"].ap()[ts * 128:(ts + 1) * 128, :], in_=OY[:, :]))

        S.run()



def _consts(nc, d, causal_masks):
    d["c_ident"] = nc.inline_tensor(np.eye(128, dtype=np.float32), name="c_ident")
    d["c_consts"] = nc.inline_tensor(
        np.broadcast_to(np.array([EPS, 1e-12], dtype=np.float32), (128, 2)).copy(),
        name="c_consts")
    d["c_consts4"] = nc.inline_tensor(
        np.broadcast_to(np.array([1.5, 3.5, 1e-20, 2.5], dtype=np.float32),
                        (128, 4)).copy(), name="c_consts4")
    d["c_eps"] = nc.inline_tensor(
        np.broadcast_to(np.array([EPS], dtype=np.float32), (128, 1)).copy(),
        name="c_eps")


def build_A():
    nc = bass.Bass("TRN2", target_bir_lowering=False)
    d = {}
    di = lambda n, sh, dt=F32: d.__setitem__(n, nc.dram_tensor(n, list(sh), dt, kind="ExternalInput"))
    do = lambda n, sh, dt=F32: d.__setitem__(n, nc.dram_tensor(n, list(sh), dt, kind="ExternalOutput"))
    di("x", [T, D])
    di("lnw_r", [128, D]); di("lnb_r", [128, D])
    di("qkvwT", [D, 3 * D]); di("qkvb_r", [128, 3 * D])
    di("gate_wT", [D, N_EXP]); di("ebias_r", [128, N_EXP])
    di("gpT", [N_EXP, D, INTER]); di("upT", [N_EXP, D, INTER]); di("dpT", [N_EXP, INTER, D])
    di("sgT", [D, INTER]); di("suT", [D, INTER]); di("sdT", [INTER, D])
    do("xn0", [T, D]); do("fy", [T, D])
    do("qT0", [D, T], BF16); do("kT0", [D, T], BF16); do("v0", [T, D], BF16)
    _consts(nc, d, False)
    with ExitStack() as octx:
        sems, semval = make_sems(nc, octx)
        emit_p1(nc, d, sems, semval)
        emit_ffn0(nc, d, sems, semval)
        final_barrier(nc, sems, semval)
    return nc


def build_B():
    nc = bass.Bass("TRN2", target_bir_lowering=False)
    d = {}
    di = lambda n, sh, dt=F32: d.__setitem__(n, nc.dram_tensor(n, list(sh), dt, kind="ExternalInput"))
    do = lambda n, sh, dt=F32: d.__setitem__(n, nc.dram_tensor(n, list(sh), dt, kind="ExternalOutput"))
    di("x", [T, D]); di("xn0", [T, D]); di("fy", [T, D])
    di("qT0", [D, T], BF16); di("kTg0", [D, SEQ], BF16); di("vg0", [SEQ, D], BF16)
    di("owT0", [D, D]); di("outb_r", [128, D])
    di("wq", [D, D]); di("wk", [D, D]); di("wv", [D, D])
    d["s0"] = nc.dram_tensor("s0", [T, D], F32, kind="Internal")
    do("x1", [T, D]); do("xn1", [T, D])
    do("qT1", [D, T], BF16); do("kT1", [D, T], BF16); do("v1", [T, D], BF16)
    _consts(nc, d, False)
    with ExitStack() as octx:
        sems, semval = make_sems(nc, octx)
        emit_attn(nc, d, sems, semval, causal=False)
        emit_comb(nc, d, sems, semval)
        final_barrier(nc, sems, semval)
    return nc


def build_C():
    nc = bass.Bass("TRN2", target_bir_lowering=False)
    d = {}
    di = lambda n, sh, dt=F32: d.__setitem__(n, nc.dram_tensor(n, list(sh), dt, kind="ExternalInput"))
    di("x1", [T, D]); di("xn1", [T, D])
    di("qT1", [D, T], BF16); di("kTg1", [D, SEQ], BF16); di("vg1", [SEQ, D], BF16)
    di("owT1", [D, D]); di("mkt", [128, 16 * 512], BF16)
    di("w1", [D, L1H]); di("b1_r", [128, L1H]); di("w2", [L1H, D]); di("b2_r", [128, D])
    d["s1"] = nc.dram_tensor("s1", [T, D], F32, kind="Internal")
    d["out"] = nc.dram_tensor("out", [T, D], F32, kind="ExternalOutput")
    _consts(nc, d, True)
    with ExitStack() as octx:
        sems, semval = make_sems(nc, octx)
        emit_attn(nc, d, sems, semval, causal=True)
        emit_p5(nc, d, sems, semval)
        final_barrier(nc, sems, semval)
    return nc


def _rep(vec):
    v = np.ascontiguousarray(np.asarray(vec, dtype=np.float32)).reshape(1, -1)
    return np.ascontiguousarray(np.broadcast_to(v, (128, v.shape[1])))


def kernel(**inp):
    _ = inp["L1_ffn_w_gate"], inp["L1_ffn_w_noise"]
    x = np.ascontiguousarray(inp["L0_attn_x"], dtype=np.float32)
    cores = list(range(8))
    shards = [np.ascontiguousarray(x[i // 4, (i % 4) * T:(i % 4 + 1) * T, :]) for i in cores]

    ncA = build_A()
    cA = {
        "lnw_r": _rep(inp["L0_attn_ln_w"]),
        "lnb_r": _rep(inp["L0_attn_ln_b"]),
        "qkvwT": np.ascontiguousarray(inp["L0_attn_qkv_w"].T),
        "qkvb_r": _rep(inp["L0_attn_qkv_b"]),
        "gate_wT": np.ascontiguousarray(inp["L0_ffn_gate_w"].T),
        "ebias_r": _rep(inp["L0_ffn_e_bias"]),
        "gpT": np.ascontiguousarray(inp["L0_ffn_gate_proj"].transpose(0, 2, 1)),
        "upT": np.ascontiguousarray(inp["L0_ffn_up_proj"].transpose(0, 2, 1)),
        "dpT": np.ascontiguousarray(inp["L0_ffn_down_proj"].transpose(0, 2, 1)),
        "sgT": np.ascontiguousarray(inp["L0_ffn_shared_gate_w"].T),
        "suT": np.ascontiguousarray(inp["L0_ffn_shared_up_w"].T),
        "sdT": np.ascontiguousarray(inp["L0_ffn_shared_down_w"].T),
    }
    rA = run_bass_kernel_spmd(ncA, [dict(cA, x=shards[i]) for i in cores],
                              core_ids=cores).results
    kTg0 = [np.ascontiguousarray(np.concatenate([rA[b * 4 + r]["kT0"] for r in range(4)], axis=1))
            for b in range(2)]
    vg0 = [np.ascontiguousarray(np.concatenate([rA[b * 4 + r]["v0"] for r in range(4)], axis=0))
           for b in range(2)]

    ncB = build_B()
    cB = {
        "owT0": np.ascontiguousarray(inp["L0_attn_out_w"].T),
        "outb_r": _rep(inp["L0_attn_out_b"]),
        "wq": np.ascontiguousarray(inp["L1_attn_Q"].transpose(1, 2, 0).reshape(D, D)),
        "wk": np.ascontiguousarray(inp["L1_attn_K"].transpose(1, 2, 0).reshape(D, D)),
        "wv": np.ascontiguousarray(inp["L1_attn_V"].transpose(1, 2, 0).reshape(D, D)),
    }
    rB = run_bass_kernel_spmd(
        ncB, [dict(cB, x=shards[i], xn0=rA[i]["xn0"], fy=rA[i]["fy"],
                   qT0=rA[i]["qT0"], kTg0=kTg0[i // 4], vg0=vg0[i // 4])
              for i in cores], core_ids=cores).results
    kTg1 = [np.ascontiguousarray(np.concatenate([rB[b * 4 + r]["kT1"] for r in range(4)], axis=1))
            for b in range(2)]
    vg1 = [np.ascontiguousarray(np.concatenate([rB[b * 4 + r]["v1"] for r in range(4)], axis=0))
           for b in range(2)]

    ncC = build_C()
    cC = {
        "owT1": np.ascontiguousarray(inp["L1_attn_W"].transpose(2, 1, 0).reshape(D, D)),
        "w1": np.ascontiguousarray(inp["L1_ffn_W1"][1]),
        "b1_r": _rep(inp["L1_ffn_b1"][1]),
        "w2": np.ascontiguousarray(inp["L1_ffn_W2"][1]),
        "b2_r": _rep(inp["L1_ffn_b2"][1]),
    }
    p_ = np.arange(128, dtype=np.float32).reshape(128, 1)
    c_ = np.arange(512, dtype=np.float32).reshape(1, 512)
    mkts = []
    for r in range(4):
        m = np.zeros((128, 16 * 512), dtype=np.float32)
        for t in range(16):
            dk = t - 4 * r
            if dk < 0:
                m[:, t * 512:(t + 1) * 512] = 1.0
            elif dk <= 3:
                m[:, t * 512:(t + 1) * 512] = (c_ >= dk * 128 + p_)
        mkts.append(m.astype(ml_dtypes.bfloat16))

    def csel(key, axis, i):
        b, r = i // 4, i % 4
        if axis == 1:
            parts = [rB[b * 4 + k][key][:, r * 512:(r + 1) * 512] for k in range(4)]
        else:
            parts = [rB[b * 4 + k][key][r * 512:(r + 1) * 512, :] for k in range(4)]
        return np.ascontiguousarray(np.concatenate(parts, axis=axis))

    rC = run_bass_kernel_spmd(
        ncC, [dict(cC, x1=csel("x1", 0, i), xn1=csel("xn1", 0, i),
                   qT1=csel("qT1", 1, i),
                   kTg1=kTg1[i // 4], vg1=vg1[i // 4], mkt=mkts[i % 4])
              for i in cores], core_ids=cores).results

    out = np.empty((2, SEQ, D), dtype=np.float32)
    for i in cores:
        b, r = i // 4, i % 4
        for k in range(4):
            out[b, (4 * k + r) * 512:(4 * k + r) * 512 + 512, :] = \
                rC[i]["out"][k * 512:(k + 1) * 512, :]
    return out
